# version 05 (final)
# baseline (speedup 1.0000x reference)
import os
import numpy as np
import ml_dtypes
import jax
try:
    jax.config.update("jax_compilation_cache_dir", "/root/.jax_bass_cache")
    jax.config.update("jax_persistent_cache_min_compile_time_secs", 0.0)
    jax.config.update("jax_persistent_cache_min_entry_size_bytes", -1)
except Exception:
    pass
import concourse.bass as bass
import concourse.mybir as mybir
from concourse.bass_utils import run_bass_kernel_spmd

F32 = mybir.dt.float32
BF16 = mybir.dt.bfloat16
U8 = mybir.dt.uint8
AF = mybir.ActivationFunctionType
OP = mybir.AluOpType

T = 4096
D = 1024
NPT = 8
CH = 256
NCH = T // CH
NL = 8
FF = 4096
NFT = FF // 128

GC = 0.7978845608028654
GA = 0.044715 * GC
EPS_IN = 1e-6
EPS_OUT = 1e-5
ATT_EPS = 1e-6
RMS_EPS = 1e-8
SCALE_ATT = 0.125

ENGS = ("sync", "tensor", "scalar", "vector", "gpsimd")

A_WE = 0
A_WF = 65536
A_ZIN0 = 131072
A_ZIN1 = 139264
A_XN = 147456
A_XNB = 155648
A_ZBF = 159744
A_ZO0 = 161792
A_ZO1 = 169984
A_HB = 178176
A_WK = 194560
A_END = 200704
ARENA_BYTES = A_END


def _v(t, byte_off, p0, p1, shape, dtype):
    eb = mybir.dt.size(dtype)
    assert byte_off % eb == 0
    base = byte_off // eb
    n = 1
    for s in shape:
        n *= s
    ap = t.bitcast(dtype)[p0:p1, base:base + n]
    if len(shape) > 1:
        ap = ap.rearrange("p (a b) -> p a b", b=shape[-1])
    return ap


class Ctx:
    def __init__(self, nc):
        self.nc = nc
        self.c = {k: 0 for k in ("LD", "WT", "ST", "PE", "EV", "DV", "GP")}
        self.sems = {}
        self.bsems = {}
        self.phases = []
        self.cur = None
        self.strip_rows = 0

    def new_phase(self, name):
        if self.cur is not None:
            self.cur["end"] = dict(self.c)
        ph = {e: [] for e in ENGS}
        ph["name"] = name
        ph["idx"] = len(self.phases)
        self.phases.append(ph)
        self.cur = ph
        return ph

    def finalize(self):
        if self.cur is not None:
            self.cur["end"] = dict(self.c)

    def ins(self, eng_name, fn):
        self.cur[eng_name].append(fn)

    def wait(self, eng_name, sem_name, val):
        if val is None or val <= 0:
            return
        s = self.sems[sem_name]
        self.ins(eng_name, lambda eng, s=s, v=val: eng.wait_ge(s, v))

    def dma(self, eng_name, out_ap, in_ap, sem_name, **kw):
        s = self.sems[sem_name]
        self.c[sem_name] += 16
        self.ins(eng_name,
                 lambda eng, o=out_ap, i=in_ap, s=s, kw=kw:
                 eng.dma_start(out=o, in_=i, **kw).then_inc(s, 16))
        return self.c[sem_name]

    def run(self, eng_name, sem_name, fn):
        s = self.sems[sem_name]
        self.c[sem_name] += 1
        self.ins(eng_name, lambda eng, f=fn, s=s: f(eng).then_inc(s, 1))
        return self.c[sem_name]

    def strip_row(self):
        r = self.strip_rows
        self.strip_rows += 1
        assert r < 32
        return r


def host_layout(inputs):
    g = lambda k: np.asarray(inputs[k])
    out = {}
    x = g("L0_attn_x")
    out["xt"] = [np.ascontiguousarray(x[b].T) for b in range(8)]
    out["l0_wq"] = np.ascontiguousarray(g("L0_attn_Wq").transpose(0, 2, 1, 3).reshape(8, D, D))
    out["l0_wk"] = np.ascontiguousarray(g("L0_attn_Wk").transpose(0, 2, 1, 3).reshape(8, D, D))
    out["l0_wv"] = np.ascontiguousarray(g("L0_attn_Wv").transpose(0, 2, 1, 3).reshape(8, D, D))
    out["l0_wo"] = np.ascontiguousarray(g("L0_attn_Wo"))
    out["l0_w1"] = np.ascontiguousarray(g("L0_attn_W1"))
    out["l0_w2"] = np.ascontiguousarray(g("L0_attn_W2"))

    def pv(a):
        return np.ascontiguousarray(a.reshape(8, NPT, 128).transpose(0, 2, 1))

    out["l0_bq"] = pv(g("L0_attn_bq").reshape(8, D))
    out["l0_bk"] = pv(g("L0_attn_bk").reshape(8, D))
    out["l0_bv"] = pv(g("L0_attn_bv").reshape(8, D))
    out["l0_bo"] = pv(g("L0_attn_bo"))
    out["l0_l1s"] = pv(g("L0_attn_ln1_s"))
    out["l0_l1b"] = pv(g("L0_attn_ln1_b"))
    out["l0_l2s"] = pv(g("L0_attn_ln2_s"))
    out["l0_l2b"] = pv(g("L0_attn_ln2_b"))
    out["l0_b1"] = np.ascontiguousarray(g("L0_attn_b1").reshape(8, 32, 128).transpose(0, 2, 1))
    out["l0_b2"] = pv(g("L0_attn_b2"))
    out["f_wu"] = np.ascontiguousarray(g("L0_ffn_w_up").T)
    out["f_wd"] = np.ascontiguousarray(g("L0_ffn_w_down").T)
    out["a_wq"] = np.ascontiguousarray(g("L1_attn_wq").T)
    out["a_wk"] = np.ascontiguousarray(g("L1_attn_wk").T)
    out["a_wv"] = np.ascontiguousarray(g("L1_attn_wv").T)
    out["a_wo"] = np.ascontiguousarray(g("L1_attn_wo").T)
    out["r_wr"] = np.ascontiguousarray(g("L1_ffn_W_r").T)
    out["r_wk"] = np.ascontiguousarray(g("L1_ffn_W_k").T)
    out["r_wv"] = np.ascontiguousarray(g("L1_ffn_W_v").T)
    out["r_mur"] = np.ascontiguousarray(g("L1_ffn_mu_r").reshape(NPT, 128).T)
    out["r_muk"] = np.ascontiguousarray(g("L1_ffn_mu_k").reshape(NPT, 128).T)
    return out


def emit_rstd_batch(ctx, strip_row, eps, out_is_rx=False):
    V, dr = ctx.V, ctx.dr
    st = dr["strips"]
    q_m, q_v, q_a, q_b = V["rsq"]
    last = 0
    for h in range(4):
        o = h * 1024
        if h > 0:
            ctx.wait("sync", "LD", last)
        ctx.dma("sync", q_m, st[strip_row:strip_row + 1, o:o + 1024], "LD")
        l_v = ctx.dma("sync", q_v, st[strip_row:strip_row + 1, T + o:T + o + 1024], "LD")
        ctx.wait("vector", "LD", l_v)
        if out_is_rx:
            d1 = ctx.run("vector", "DV",
                         lambda e: e.tensor_scalar(out=q_a, in0=q_v, scalar1=1.0 / D,
                                                   scalar2=None, op0=OP.mult))
            ctx.wait("scalar", "DV", d1)
            e1 = ctx.run("scalar", "EV",
                         lambda e: e.activation(out=q_b, in_=q_a, func=AF.Ln))
            ctx.wait("scalar", "EV", e1)
            e2 = ctx.run("scalar", "EV",
                         lambda e: e.activation(out=q_a, in_=q_b, func=AF.Exp, scale=0.5))
            ctx.wait("vector", "EV", e2)
            d2 = ctx.run("vector", "DV",
                         lambda e: e.tensor_scalar(out=q_b, in0=q_a, scalar1=1.0,
                                                   scalar2=RMS_EPS, op0=OP.mult, op1=OP.add))
            ctx.wait("vector", "DV", d2)
            d3 = ctx.run("vector", "DV", lambda e: e.reciprocal(out=q_a, in_=q_b))
            ctx.wait("sync", "DV", d3)
        else:
            d1 = ctx.run("vector", "DV",
                         lambda e: e.tensor_scalar(out=q_a, in0=q_m, scalar1=1.0 / D,
                                                   scalar2=None, op0=OP.mult))
            ctx.wait("vector", "DV", d1)
            d2 = ctx.run("vector", "DV",
                         lambda e: e.tensor_tensor(out=q_b, in0=q_a, in1=q_a, op=OP.mult))
            ctx.wait("vector", "DV", d2)
            d3 = ctx.run("vector", "DV",
                         lambda e: e.scalar_tensor_tensor(out=q_a, in0=q_v, scalar=1.0 / D,
                                                          in1=q_b, op0=OP.mult,
                                                          op1=OP.subtract))
            ctx.wait("vector", "DV", d3)
            d4 = ctx.run("vector", "DV",
                         lambda e, eps=eps: e.tensor_scalar(out=q_b, in0=q_a, scalar1=1.0,
                                                            scalar2=eps, op0=OP.mult,
                                                            op1=OP.add))
            ctx.wait("scalar", "DV", d4)
            e1 = ctx.run("scalar", "EV",
                         lambda e: e.activation(out=q_a, in_=q_b, func=AF.Ln))
            ctx.wait("scalar", "EV", e1)
            e2 = ctx.run("scalar", "EV",
                         lambda e: e.activation(out=q_b, in_=q_a, func=AF.Exp, scale=-0.5))
            ctx.wait("sync", "EV", e2)
        outq = q_a if out_is_rx else q_b
        last = ctx.dma("sync", st[strip_row:strip_row + 1, T + o:T + o + 1024], outq, "LD")
    return last


def chunk_stage_load(ctx, strip_row, c, after=None, after_st=0, after_dv=0):
    if after:
        ctx.wait("sync", "LD", after)
    if after_st:
        ctx.wait("sync", "ST", after_st)
    if after_dv:
        ctx.wait("sync", "DV", after_dv)
    st = ctx.dr["strips"]
    src = st[strip_row:strip_row + 1, 0:2 * T].rearrange("o (h t) -> o h t", h=2)
    return ctx.dma("sync", ctx.V["sti"][c % 2].rearrange("o (h t) -> o h t", h=2),
                   src[:, :, c * CH:(c + 1) * CH], "LD")


def emit_bcast(ctx, c, ld_mark, pe_guard=None):
    V, P = ctx.V, ctx.P
    pssb = ctx.ps["pssb"]
    sti = V["sti"][c % 2]
    ctx.wait("tensor", "LD", ld_mark)
    if pe_guard is not None:
        ctx.wait("tensor", pe_guard[0], pe_guard[1])
    ctx.run("tensor", "PE",
            lambda e: e.matmul(pssb[:, 0:CH], V["ones1f"], sti[0:1, 0:CH],
                               start=True, stop=False, skip_group_check=True))
    p = ctx.run("tensor", "PE",
                lambda e: e.matmul(pssb[:, CH:2 * CH], V["ones1f"], sti[0:1, CH:2 * CH],
                                   start=False, stop=True, skip_group_check=True))
    ctx.wait("scalar", "PE", p)
    ctx.run("scalar", "EV", lambda e: e.activation(out=P["mbc"], in_=pssb[:, 0:CH], func=AF.Copy))
    ev = ctx.run("scalar", "EV", lambda e: e.activation(out=P["rbc"], in_=pssb[:, CH:2 * CH], func=AF.Copy))
    return ev


def emit_normalize(ctx, c, ev_bc, ln, want_f32, want_bf16, zin=None):
    V, P = ctx.V, ctx.P
    zin = zin if zin is not None else V["zin"][c % 2]
    wk = V["wk"]
    dv = ev = None
    _aff_ev = {}
    ctx.wait("vector", "EV", ev_bc)
    for pt in range(NPT):
        d1 = ctx.run("vector", "DV",
                     lambda e, pt=pt, zin=zin: e.scalar_tensor_tensor(
                         out=wk[0], in0=P["mbc"], scalar=-1.0 / D, in1=zin[:, pt, :],
                         op0=OP.mult, op1=OP.add))
        ctx.wait("vector", "DV", d1)
        if ln is None:
            if want_f32:
                dv = ctx.run("vector", "DV",
                             lambda e, pt=pt: e.tensor_tensor(
                                 out=V["xn"][:, pt, :], in0=wk[0], in1=P["rbc"], op=OP.mult))
                if want_bf16:
                    ctx.wait("scalar", "DV", dv)
                    ev = ctx.run("scalar", "EV",
                                 lambda e, pt=pt: e.activation(
                                     out=V["xnb"][:, pt, :], in_=V["xn"][:, pt, :], func=AF.Copy))
            else:
                dv = ctx.run("vector", "DV",
                             lambda e, pt=pt: e.tensor_tensor(
                                 out=V["xnb"][:, pt, :], in0=wk[0], in1=P["rbc"], op=OP.mult))
        else:
            sj, bj, j = ln
            wt = wk[1 + 3 * (pt % 2)]
            g = _aff_ev.get(pt - 1)
            if g:
                ctx.wait("vector", "EV", g)
            d2 = ctx.run("vector", "DV",
                         lambda e, wt=wt: e.tensor_tensor(out=wt, in0=wk[0],
                                                          in1=P["rbc"], op=OP.mult))
            ctx.wait("scalar", "DV", d2)
            dv = d2
            if want_f32:
                e1 = ctx.run("scalar", "EV",
                             lambda e, pt=pt, sj=sj, bj=bj, j=j, wt=wt: e.activation(
                                 out=V["xn"][:, pt, :], in_=wt, func=AF.Identity,
                                 scale=P[sj][:, j * 8 + pt:j * 8 + pt + 1],
                                 bias=P[bj][:, j * 8 + pt:j * 8 + pt + 1]))
                _aff_ev[pt] = e1
                if want_bf16:
                    ctx.wait("scalar", "EV", e1)
                    ev = ctx.run("scalar", "EV",
                                 lambda e, pt=pt: e.activation(
                                     out=V["xnb"][:, pt, :], in_=V["xn"][:, pt, :], func=AF.Copy))
                else:
                    ev = e1
            else:
                ev = ctx.run("scalar", "EV",
                             lambda e, pt=pt, sj=sj, bj=bj, j=j, wt=wt: e.activation(
                                 out=V["xnb"][:, pt, :], in_=wt, func=AF.Identity,
                                 scale=P[sj][:, j * 8 + pt:j * 8 + pt + 1],
                                 bias=P[bj][:, j * 8 + pt:j * 8 + pt + 1]))
                _aff_ev[pt] = ev
    return dv, ev


def emit_stats(ctx, c, src_chunk_ap, strip_row, ready, sq_only=False, stm_prev=0):
    V = ctx.V
    pssb = ctx.ps["pssb"]
    zbf = V["zbf"]
    ctx.wait("scalar", ready[0], ready[1])
    ctx.wait("tensor", ready[0], ready[1])
    pe = None
    for pt in range(NPT):
        sl = pt % 2
        if pt >= 1:
            ctx.wait("scalar", "PE", pe)
        if not sq_only:
            ctx.run("scalar", "EV",
                    lambda e, pt=pt, sl=sl: e.activation(
                        out=zbf[:, sl, 0:CH], in_=src_chunk_ap[:, pt, :], func=AF.Copy))
        e2 = ctx.run("scalar", "EV",
                     lambda e, pt=pt, sl=sl: e.activation(
                         out=zbf[:, sl, CH:2 * CH], in_=src_chunk_ap[:, pt, :], func=AF.Square))
        ctx.wait("tensor", "EV", e2)
        pe = ctx.run("tensor", "PE",
                     lambda e, sl=sl, pt=pt: e.matmul(
                         pssb[:, 0:512], V["ones128b"], zbf[:, sl, 0:2 * CH],
                         start=(pt == 0), stop=(pt == NPT - 1), skip_group_check=True))
        if pt % 2 == 1:
            pe_prev2 = pe
        elif pt == 0:
            pe_prev2 = pe
    ctx.wait("vector", "PE", pe)
    ctx.wait("vector", "ST", stm_prev)
    dv = ctx.run("vector", "DV",
                 lambda e, c=c: e.tensor_copy(out=V["sti"][c % 2], in_=pssb[0:1, 0:512]))
    ctx.wait("scalar", "DV", dv)
    st = ctx.dr["strips"]
    dst = st[strip_row:strip_row + 1, 0:2 * T].rearrange("o (h t) -> o h t", h=2)
    stm = ctx.dma("scalar", dst[:, :, c * CH:(c + 1) * CH],
                  V["sti"][c % 2].rearrange("o (h t) -> o h t", h=2), "ST")
    return dv, stm


def load_w_small(ctx, slot, src_ap):
    dst = ctx.V["ws"][slot].rearrange("p (ct o) -> p ct o", o=D)
    src = src_ap.rearrange("(ct p) o -> p ct o", p=128)
    return ctx.dma("gpsimd", dst, src, "WT")


def load_w_big(ctx, which, src_ap, out_w):
    dst = ctx.V[which].rearrange("p (ct o) -> p ct o", o=out_w)
    src = src_ap.rearrange("(ct p) o -> p ct o", p=128)
    nct = src.shape[1]
    mark = 0
    for a in range(0, nct, 8):
        b = min(a + 8, nct)
        mark = ctx.dma("gpsimd", dst[:, a:b, :], src[:, a:b, :], "WT")
    return mark


def z_chunk_load(ctx, src, c, guard_dv=None):
    if guard_dv:
        ctx.wait("sync", "DV", guard_dv)
    s = src.rearrange("(pt p) t -> p pt t", p=128)[:, :, c * CH:(c + 1) * CH]
    return ctx.dma("sync", ctx.V["zin"][c % 2], s, "LD")


def z_chunk_store(ctx, dst, c, ready_dv=None, ready_ev=None):
    d = dst.rearrange("(pt p) t -> p pt t", p=128)[:, :, c * CH:(c + 1) * CH]
    if ready_dv:
        ctx.wait("scalar", "DV", ready_dv)
    if ready_ev:
        ctx.wait("scalar", "EV", ready_ev)
    return ctx.dma("scalar", d, ctx.V["zo"][c % 2], "ST")


def ph_prep(ctx):
    ctx.new_phase("prep")
    P, dr = ctx.P, ctx.dr
    for nm, src in [("bq", "l0_bq"), ("bk", "l0_bk"), ("bv", "l0_bv"), ("bo", "l0_bo"),
                    ("l1s", "l0_l1s"), ("l1b", "l0_l1b"), ("l2s", "l0_l2s"), ("l2b", "l0_l2b")]:
        s = dr[src].rearrange("j p m -> p j m")
        ctx.dma("sync", P[nm].rearrange("p (j m) -> p j m", m=8), s, "LD")
    ctx.dma("sync", P["b1"].rearrange("p (j m) -> p j m", m=32),
            dr["l0_b1"].rearrange("j p m -> p j m"), "LD")
    ctx.dma("sync", P["b2"].rearrange("p (j m) -> p j m", m=8),
            dr["l0_b2"].rearrange("j p m -> p j m"), "LD")
    ctx.dma("sync", P["mur"], dr["r_mur"], "LD")
    ctx.dma("sync", P["muk"], dr["r_muk"], "LD")
    ctx.run("vector", "DV", lambda e: e.memset(ctx.V["ones128b"], 1.0))
    ctx.run("vector", "DV", lambda e: e.memset(ctx.V["ones1f_all"], 1.0))
    ctx.run("vector", "DV", lambda e: e.memset(ctx.V["onescol"], 1.0))
    ctx.run("vector", "DV", lambda e: e.memset(ctx.P["pl"], 0.0))


def ph_stats0(ctx, src_name, strip_row):
    ctx.new_phase("stats0")
    V = ctx.V
    src = ctx.dr[src_name]
    ev_done = {}
    dv_x = {}
    _stm = {}
    for c in range(NCH):
        if c >= 2:
            ctx.wait("sync", "EV", ev_done[c - 1])
        ld = z_chunk_load(ctx, src, c)
        ctx.wait("scalar", "LD", ld)
        pe = None
        for pt in range(NPT):
            sl = pt % 2
            if pt >= 2:
                ctx.wait("scalar", "PE", pe)
            ctx.run("scalar", "EV",
                    lambda e, pt=pt, sl=sl, c=c: e.activation(
                        out=V["zbf"][:, sl, 0:CH], in_=V["zin"][c % 2][:, pt, :], func=AF.Copy))
            e2 = ctx.run("scalar", "EV",
                         lambda e, pt=pt, sl=sl, c=c: e.activation(
                             out=V["zbf"][:, sl, CH:2 * CH], in_=V["zin"][c % 2][:, pt, :],
                             func=AF.Square))
            ctx.wait("tensor", "EV", e2)
            if pt == 0 and c > 0:
                ctx.wait("tensor", "DV", dv_x[c - 1])
            pe = ctx.run("tensor", "PE",
                         lambda e, sl=sl, pt=pt: e.matmul(
                             ctx.ps["pssb"][:, 0:512], V["ones128b"], V["zbf"][:, sl, 0:2 * CH],
                             start=(pt == 0), stop=(pt == NPT - 1), skip_group_check=True))
        ev_done[c] = ctx.c["EV"]
        ctx.wait("vector", "PE", pe)
        ctx.wait("vector", "ST", _stm.get(c - 1, 0))
        dv = ctx.run("vector", "DV",
                     lambda e, c=c: e.tensor_copy(out=V["sti"][c % 2],
                                                  in_=ctx.ps["pssb"][0:1, 0:512]))
        dv_x[c] = dv
        ctx.wait("scalar", "DV", dv)
        st = ctx.dr["strips"]
        dst = st[strip_row:strip_row + 1, 0:2 * T].rearrange("o (h t) -> o h t", h=2)
        _stm[c] = ctx.dma("scalar", dst[:, :, c * CH:(c + 1) * CH],
                          V["sti"][c % 2].rearrange("o (h t) -> o h t", h=2), "ST")
    ctx.wait("scalar", "ST", ctx.c["ST"])


def ph_A(ctx, j, src_name, strip_in, ln, do_batch):
    ctx.new_phase(f"A{j}")
    V, P, dr = ctx.V, ctx.P, ctx.dr
    rstd_end = emit_rstd_batch(ctx, strip_in, EPS_OUT if ln is None else EPS_IN) if do_batch else 0
    load_w_small(ctx, 0, dr["l0_wk"][j])
    wWv = load_w_small(ctx, 1, dr["l0_wv"][j])
    src = dr[src_name]
    psK = [ctx.ps["ps0"], ctx.ps["ps1"]]
    psV = ctx.ps["psw0"]
    wk0 = V["ws"][0].rearrange("p (ct o) -> p ct o", o=D)
    wv0 = V["ws"][1].rearrange("p (ct o) -> p ct o", o=D)
    norm_dv = {}
    bc_g = {}
    ev_a = {}
    prev_e1 = 0
    for c in range(NCH):
        ld = z_chunk_load(ctx, src, c, guard_dv=norm_dv.get(c - 1))
        lds = chunk_stage_load(ctx, strip_in, c, after=rstd_end if c == 0 else None,
                               after_dv=norm_dv.get(c - 1, 0))
        evbc = emit_bcast(ctx, c, lds,
                          pe_guard=None if c == 0 else ("DV", bc_g[c - 1]))
        ctx.wait("vector", "LD", ld)
        dvn, _ = emit_normalize(ctx, c, evbc, ln, want_f32=False, want_bf16=(ln is None))
        if ln is not None:
            pass
        norm_dv[c] = dvn
        bc_g[c] = dvn
        ctx.wait("tensor", "WT", wWv)
        ctx.wait("tensor", "EV", ctx.c["EV"])
        ctx.wait("tensor", "DV", dvn)
        wk = V["wk"]
        for m in range(NPT):
            par = m % 2
            g = ev_a.get(c * NPT + m - 1)
            if g:
                ctx.wait("tensor", "DV", g)
            for ct in range(NPT):
                ctx.run("tensor", "PE",
                        lambda e, m=m, ct=ct, par=par: e.matmul(
                            psK[par][:, 0:CH], wk0[:, ct, m * 128:(m + 1) * 128],
                            V["xnb"][:, ct, :], start=(ct == 0), stop=(ct == NPT - 1),
                            skip_group_check=True))
            peK = ctx.c["PE"]
            for ct in range(NPT):
                ctx.run("tensor", "PE",
                        lambda e, m=m, ct=ct, par=par: e.matmul(
                            psV[:, par * 512:par * 512 + CH], wv0[:, ct, m * 128:(m + 1) * 128],
                            V["xnb"][:, ct, :], start=(ct == 0), stop=(ct == NPT - 1),
                            skip_group_check=True))
            peV = ctx.c["PE"]
            ctx.wait("vector", "PE", peK)
            ctx.wait("vector", "EV", prev_e1)
            d1 = ctx.run("vector", "DV",
                         lambda e, m=m, par=par: e.tensor_scalar(
                             out=wk[1], in0=psK[par][:, 0:CH],
                             scalar1=P["bk"][:, j * 8 + m:j * 8 + m + 1], scalar2=0.0,
                             op0=OP.add, op1=OP.min))
            ctx.wait("scalar", "DV", d1)
            prev_e1 = ctx.run("scalar", "EV",
                              lambda e: e.activation(out=wk[2], in_=wk[1], func=AF.Exp))
            ctx.wait("scalar", "PE", peK)
            e2 = ctx.run("scalar", "EV",
                         lambda e, m=m, par=par: e.activation(
                             out=wk[4], in_=psK[par][:, 0:CH], func=AF.Relu,
                             bias=P["bk"][:, j * 8 + m:j * 8 + m + 1]))
            ctx.wait("vector", "EV", e2)
            ctx.run("vector", "DV",
                    lambda e, m=m, c=c: e.scalar_tensor_tensor(
                        out=wk[0], in0=wk[2], scalar=0.0, in1=wk[4],
                        op0=OP.add, op1=OP.add,
                        accum_out=P["skp"][:, m * 16 + c:m * 16 + c + 1]))
            ctx.wait("vector", "PE", peV)
            d3 = ctx.run("vector", "DV",
                         lambda e, m=m, par=par: e.tensor_scalar(
                             out=wk[3], in0=psV[:, par * 512:par * 512 + CH],
                             scalar1=P["bv"][:, j * 8 + m:j * 8 + m + 1], scalar2=None,
                             op0=OP.add))
            ctx.wait("vector", "DV", d3)
            d4 = ctx.run("vector", "DV",
                         lambda e, m=m, c=c: e.scalar_tensor_tensor(
                             out=wk[5], in0=wk[0], scalar=1.0, in1=wk[3],
                             op0=OP.mult, op1=OP.mult,
                             accum_out=P["kvp"][:, m * 16 + c:m * 16 + c + 1]))
            ev_a[c * NPT + m] = d4
    for m in range(NPT):
        ctx.run("vector", "DV",
                lambda e, m=m: e.reduce_sum(out=P["sk"][:, m:m + 1],
                                            in_=P["skp"][:, m * 16:(m + 1) * 16],
                                            axis=mybir.AxisListType.X))
        ctx.run("vector", "DV",
                lambda e, m=m: e.reduce_sum(out=P["kv"][:, m:m + 1],
                                            in_=P["kvp"][:, m * 16:(m + 1) * 16],
                                            axis=mybir.AxisListType.X))


def ph_B1(ctx, j, src_name, strip_in, ln, strip_out, out_name):
    ctx.new_phase(f"B1_{j}")
    V, P, dr = ctx.V, ctx.P, ctx.dr
    load_w_small(ctx, 2, dr["l0_wq"][j])
    wWo = load_w_small(ctx, 3, dr["l0_wo"][j])
    src = dr[src_name]
    dst = dr[out_name]
    psQ = [ctx.ps["ps0"], ctx.ps["ps1"]]
    psY = ctx.ps["psw0"]
    wq2 = V["ws"][2].rearrange("p (ct o) -> p ct o", o=D)
    wo3 = V["ws"][3].rearrange("p (ct o) -> p ct o", o=D)
    st = {}
    for c in range(NCH):
        ld = z_chunk_load(ctx, src, c, guard_dv=st.get(("norm", c - 1)))
        lds = chunk_stage_load(ctx, strip_in, c, after_st=st.get(("stm", c - 1), 0))
        evbc = emit_bcast(ctx, c, lds,
                          pe_guard=None if c == 0 else ("DV", st[("stx", c - 1)]))
        ctx.wait("vector", "LD", ld)
        dvn, evn = emit_normalize(ctx, c, evbc, ln, want_f32=True, want_bf16=True)
        st[("norm", c)] = dvn
        wk = V["wk"]
        ctx.wait("tensor", "WT", wWo)
        ctx.wait("tensor", "EV", evn)
        zo = V["zo"][c % 2]
        ctx.wait("vector", "ST", st.get(("zst", c - 1)))
        for m in range(NPT):
            par = m % 2
            g = st.get(("attb", c * NPT + m - 1))
            if g:
                ctx.wait("tensor", "DV", g)
            for ct in range(NPT):
                ctx.run("tensor", "PE",
                        lambda e, m=m, ct=ct, par=par: e.matmul(
                            psQ[par][:, 0:CH], wq2[:, ct, m * 128:(m + 1) * 128],
                            V["xnb"][:, ct, :], start=(ct == 0), stop=(ct == NPT - 1),
                            skip_group_check=True))
            peQ = ctx.c["PE"]
            ctx.wait("vector", "PE", peQ)
            d1 = ctx.run("vector", "DV",
                         lambda e, m=m, par=par: e.tensor_scalar(
                             out=wk[1], in0=psQ[par][:, 0:CH],
                             scalar1=P["bq"][:, j * 8 + m:j * 8 + m + 1], scalar2=0.0,
                             op0=OP.add, op1=OP.min))
            ctx.wait("scalar", "DV", d1)
            e1 = ctx.run("scalar", "EV",
                         lambda e: e.activation(out=wk[2], in_=wk[1], func=AF.Exp))
            ctx.wait("scalar", "PE", peQ)
            e2 = ctx.run("scalar", "EV",
                         lambda e, m=m, par=par: e.activation(
                             out=wk[4], in_=psQ[par][:, 0:CH], func=AF.Relu,
                             bias=P["bq"][:, j * 8 + m:j * 8 + m + 1]))
            ctx.wait("gpsimd", "EV", e2)
            g1 = ctx.run("gpsimd", "GP",
                         lambda e: e.tensor_tensor(out=wk[0], in0=wk[2], in1=wk[4], op=OP.add))
            ctx.wait("vector", "GP", g1)
            d2 = ctx.run("vector", "DV",
                         lambda e, m=m: e.tensor_scalar(
                             out=wk[1], in0=wk[0], scalar1=P["sk"][:, m:m + 1],
                             scalar2=ATT_EPS, op0=OP.mult, op1=OP.add))
            ctx.wait("vector", "DV", d2)
            d3 = ctx.run("vector", "DV", lambda e: e.reciprocal(out=wk[4], in_=wk[1]))
            ctx.wait("gpsimd", "DV", d3)
            g2 = ctx.run("gpsimd", "GP",
                         lambda e: e.tensor_tensor(out=wk[1], in0=wk[0], in1=wk[4], op=OP.mult))
            ctx.wait("vector", "GP", g2)
            d4 = ctx.run("vector", "DV",
                         lambda e, m=m: e.tensor_scalar(
                             out=V["attb"][:, m, :], in0=wk[1],
                             scalar1=P["kv"][:, m:m + 1], scalar2=None, op0=OP.mult))
            st[("attb", c * NPT + m)] = d4
        ctx.wait("tensor", "DV", st[("attb", c * NPT + NPT - 1)])
        z1_last = None
        for m2 in range(NPT):
            par = m2 % 2
            g = st.get(("z1", c * NPT + m2 - 1))
            if g:
                ctx.wait("tensor", "DV", g)
            for ct in range(NPT):
                ctx.run("tensor", "PE",
                        lambda e, m2=m2, ct=ct, par=par: e.matmul(
                            psY[:, par * 512:par * 512 + CH], wo3[:, ct, m2 * 128:(m2 + 1) * 128],
                            V["attb"][:, ct, :], start=(ct == 0), stop=(ct == NPT - 1),
                            skip_group_check=True))
            peY = ctx.c["PE"]
            wpar = m2 % 2
            u, a, b = wk[3 * wpar + 0], wk[3 * wpar + 1], wk[3 * wpar + 2]
            ctx.wait("scalar", "PE", peY)
            g2b = st.get(("z1", c * NPT + m2 - 1))
            if g2b:
                ctx.wait("scalar", "DV", g2b)
            ctx.run("scalar", "EV",
                    lambda e, m2=m2, par=par, u=u: e.activation(
                        out=u, in_=psY[:, par * 512:par * 512 + CH], func=AF.Identity,
                        bias=P["bo"][:, j * 8 + m2:j * 8 + m2 + 1]))
            e2 = ctx.run("scalar", "EV",
                         lambda e, m2=m2, par=par, a=a: e.activation(
                             out=a, in_=psY[:, par * 512:par * 512 + CH], func=AF.Square,
                             bias=P["bo"][:, j * 8 + m2:j * 8 + m2 + 1]))
            ctx.wait("vector", "EV", e2)
            d5 = ctx.run("vector", "DV",
                         lambda e, a=a, b=b: e.tensor_scalar(
                             out=b, in0=a, scalar1=GA, scalar2=GC, op0=OP.mult, op1=OP.add))
            ctx.wait("vector", "DV", d5)
            d6 = ctx.run("vector", "DV",
                         lambda e, a=a, b=b, u=u: e.tensor_tensor(out=a, in0=b, in1=u, op=OP.mult))
            ctx.wait("scalar", "DV", d6)
            e3 = ctx.run("scalar", "EV",
                         lambda e, a=a, b=b: e.activation(out=b, in_=a, func=AF.Tanh))
            ctx.wait("vector", "EV", e3)
            d7 = ctx.run("vector", "DV",
                         lambda e, a=a, b=b, u=u: e.scalar_tensor_tensor(
                             out=a, in0=b, scalar=1.0, in1=u, op0=OP.add, op1=OP.mult))
            ctx.wait("vector", "DV", d7)
            d8 = ctx.run("vector", "DV",
                         lambda e, a=a, m2=m2, zo=zo: e.scalar_tensor_tensor(
                             out=zo[:, m2, :], in0=a, scalar=0.5, in1=V["xn"][:, m2, :],
                             op0=OP.mult, op1=OP.add))
            st[("z1", c * NPT + m2)] = d8
            z1_last = d8
        dvx, stmk = emit_stats(ctx, c, zo, strip_out, ready=("DV", z1_last),
                               stm_prev=st.get(("stm", c - 1), 0))
        st[("stx", c)] = dvx
        st[("stm", c)] = stmk
        st[("zst", c)] = z_chunk_store(ctx, dst, c, ready_dv=z1_last)
    ctx.wait("scalar", "ST", ctx.c["ST"])


def ph_B2(ctx, j, src_name, strip_in, strip_out, out_name):
    ctx.new_phase(f"B2_{j}")
    V, P, dr = ctx.V, ctx.P, ctx.dr
    rstd_end = emit_rstd_batch(ctx, strip_in, EPS_IN)
    w1m = load_w_big(ctx, "wE_b", dr["l0_w1"][j], FF)
    w2m = load_w_big(ctx, "wF_b", dr["l0_w2"][j], D)
    src = dr[src_name]
    dst = dr[out_name]
    ln = ("l1s", "l1b", j)
    psw = [ctx.ps["psw0"], ctx.ps["psw1"]]
    psd = [ctx.ps["ps0"], ctx.ps["ps1"]]
    w1v = V["wE_b"].rearrange("p (ct o) -> p ct o", o=FF)
    w2v = V["wF_b"].rearrange("p (ct o) -> p ct o", o=D)
    st = {}
    for c in range(NCH):
        ld = z_chunk_load(ctx, src, c, guard_dv=st.get(("norm", c - 1)))
        lds = chunk_stage_load(ctx, strip_in, c, after=rstd_end if c == 0 else None,
                               after_st=st.get(("stm", c - 1), 0))
        evbc = emit_bcast(ctx, c, lds,
                          pe_guard=None if c == 0 else ("DV", st[("stx", c - 1)]))
        ctx.wait("vector", "LD", ld)
        dvn, evn = emit_normalize(ctx, c, evbc, ln, want_f32=True, want_bf16=True)
        st[("norm", c)] = dvn
        if os.environ.get("KDBG") == "x1":
            zo = V["zo"][c % 2]
            ctx.wait("vector", "ST", st.get(("zst", c - 1), 0))
            ctx.wait("vector", "DV", st.get(("stx", c - 2), 0))
            ctx.wait("vector", "EV", evn)
            dvl = None
            for pt in range(NPT):
                dvl = ctx.run("vector", "DV",
                              lambda e, pt=pt, zo=zo: e.tensor_copy(
                                  out=zo[:, pt, :], in_=V["xn"][:, pt, :]))
            dvx, stmk = emit_stats(ctx, c, zo, strip_out, ready=("DV", dvl),
                                   stm_prev=st.get(("stm", c - 1), 0))
            st[("stx", c)] = dvx
            st[("stm", c)] = stmk
            st[("zst", c)] = z_chunk_store(ctx, dst, c, ready_dv=dvl)
            st[("hev", (c + 1) * NPT - 1)] = dvl
            st[("h2ev", c * 2 + 1)] = dvl
            continue
        ctx.wait("tensor", "WT", w1m)
        ctx.wait("tensor", "EV", evn)
        if c == 0:
            ctx.wait("vector", "LD", rstd_end)
        for fp in range(NPT):
            par = fp % 2
            g = st.get(("hev", c * NPT + fp - 1))
            if g:
                ctx.wait("tensor", "DV", g)
            for ot in range(4):
                fo = fp * 4 + ot
                for ct in range(NPT):
                    ctx.run("tensor", "PE",
                            lambda e, fo=fo, ct=ct, par=par, ot=ot: e.matmul(
                                psw[par][:, ot * CH:(ot + 1) * CH],
                                w1v[:, ct, fo * 128:(fo + 1) * 128],
                                V["xnb"][:, ct, :],
                                start=(ct == 0 and ot % 2 == 0),
                                stop=(ct == NPT - 1 and ot % 2 == 1),
                                skip_group_check=True))
            peh = ctx.c["PE"]
            lasts = {}
            for ot in range(4):
                fo = fp * 4 + ot
                wpar = ot % 2
                u, a, b = V["wk"][3 * wpar + 0], V["wk"][3 * wpar + 1], V["wk"][3 * wpar + 2]
                ctx.wait("scalar", "PE", peh)
                g2 = lasts.get(ot - 1)
                if g2:
                    ctx.wait("scalar", "DV", g2)
                ctx.run("scalar", "EV",
                        lambda e, fo=fo, par=par, ot=ot, u=u: e.activation(
                            out=u, in_=psw[par][:, ot * CH:(ot + 1) * CH], func=AF.Identity,
                            bias=P["b1"][:, j * 32 + fo:j * 32 + fo + 1]))
                e2 = ctx.run("scalar", "EV",
                             lambda e, fo=fo, par=par, ot=ot, a=a: e.activation(
                                 out=a, in_=psw[par][:, ot * CH:(ot + 1) * CH], func=AF.Square,
                                 bias=P["b1"][:, j * 32 + fo:j * 32 + fo + 1]))
                ctx.wait("vector", "EV", e2)
                d1 = ctx.run("vector", "DV",
                             lambda e, a=a, b=b: e.tensor_scalar(
                                 out=b, in0=a, scalar1=GA, scalar2=GC, op0=OP.mult, op1=OP.add))
                ctx.wait("vector", "DV", d1)
                d2 = ctx.run("vector", "DV",
                             lambda e, a=a, b=b, u=u: e.tensor_tensor(out=a, in0=b, in1=u,
                                                                      op=OP.mult))
                ctx.wait("scalar", "DV", d2)
                e3 = ctx.run("scalar", "EV",
                             lambda e, a=a, b=b: e.activation(out=b, in_=a, func=AF.Tanh))
                ctx.wait("vector", "EV", e3)
                d3 = ctx.run("vector", "DV",
                             lambda e, b=b, u=u, fo=fo: e.scalar_tensor_tensor(
                                 out=V["hb"][:, fo, :], in0=b, scalar=1.0, in1=u,
                                 op0=OP.add, op1=OP.mult))
                lasts[ot] = d3
            st[("hev", c * NPT + fp)] = lasts[3]
        ctx.wait("tensor", "DV", st[("hev", c * NPT + NPT - 1)])
        ctx.wait("tensor", "WT", w2m)
        zo = V["zo"][c % 2]
        ctx.wait("vector", "ST", st.get(("zst", c - 1)))
        z2_last = None
        for r in range(2):
            g = st.get(("h2ev", c * 2 + r - 1))
            if g:
                ctx.wait("tensor", "DV", g)
        for r in range(2):
            if r == 1:
                ctx.wait("tensor", "DV", st[("h2ev", c * 2 + 0)])
            for dt4 in range(4):
                dtile = r * 4 + dt4
                pd = psd[dt4 // 2]
                off = (dt4 % 2) * CH
                for ct in range(NFT):
                    ctx.run("tensor", "PE",
                            lambda e, dtile=dtile, ct=ct, pd=pd, off=off, dt4=dt4: e.matmul(
                                pd[:, off:off + CH],
                                w2v[:, ct, dtile * 128:(dtile + 1) * 128],
                                V["hb"][:, ct, :],
                                start=(ct == 0 and dt4 % 2 == 0),
                                stop=(ct == NFT - 1 and dt4 % 2 == 1),
                                skip_group_check=True))
            peh2 = ctx.c["PE"]
            lastd = {}
            for dt4 in range(4):
                dtile = r * 4 + dt4
                pd = psd[dt4 // 2]
                off = (dt4 % 2) * CH
                wpar = dt4 % 2
                u, a, b = V["wk"][3 * wpar + 0], V["wk"][3 * wpar + 1], V["wk"][3 * wpar + 2]
                ctx.wait("scalar", "PE", peh2)
                g2 = lastd.get(dt4 - 1)
                if g2:
                    ctx.wait("scalar", "DV", g2)
                ctx.run("scalar", "EV",
                        lambda e, dtile=dtile, pd=pd, off=off, u=u: e.activation(
                            out=u, in_=pd[:, off:off + CH], func=AF.Identity, scale=0.5,
                            bias=P["b2"][:, j * 8 + dtile:j * 8 + dtile + 1]))
                e2 = ctx.run("scalar", "EV",
                             lambda e, dtile=dtile, pd=pd, off=off, a=a: e.activation(
                                 out=a, in_=pd[:, off:off + CH], func=AF.Square, scale=0.5,
                                 bias=P["b2"][:, j * 8 + dtile:j * 8 + dtile + 1]))
                ctx.wait("vector", "EV", e2)
                d1 = ctx.run("vector", "DV",
                             lambda e, a=a, b=b: e.tensor_scalar(
                                 out=b, in0=a, scalar1=GA, scalar2=GC, op0=OP.mult, op1=OP.add))
                ctx.wait("vector", "DV", d1)
                d2 = ctx.run("vector", "DV",
                             lambda e, a=a, b=b, u=u: e.tensor_tensor(out=a, in0=b, in1=u,
                                                                      op=OP.mult))
                ctx.wait("scalar", "DV", d2)
                e3 = ctx.run("scalar", "EV",
                             lambda e, a=a, b=b: e.activation(out=b, in_=a, func=AF.Tanh))
                ctx.wait("vector", "EV", e3)
                d3 = ctx.run("vector", "DV",
                             lambda e, a=a, b=b, u=u: e.scalar_tensor_tensor(
                                 out=a, in0=b, scalar=1.0, in1=u,
                                 op0=OP.add, op1=OP.mult))
                ctx.wait("vector", "DV", d3)
                d4 = ctx.run("vector", "DV",
                             lambda e, a=a, dtile=dtile, zo=zo: e.scalar_tensor_tensor(
                                 out=zo[:, dtile, :], in0=a, scalar=0.5,
                                 in1=V["xn"][:, dtile, :], op0=OP.mult, op1=OP.add))
                lastd[dt4] = d4
                z2_last = d4
            st[("h2ev", c * 2 + r)] = z2_last
        dvx, stmk = emit_stats(ctx, c, zo, strip_out, ready=("DV", z2_last),
                               stm_prev=st.get(("stm", c - 1), 0))
        st[("stx", c)] = dvx
        st[("stm", c)] = stmk
        st[("zst", c)] = z_chunk_store(ctx, dst, c, ready_dv=z2_last)
    ctx.wait("scalar", "ST", ctx.c["ST"])


def ph_normstore(ctx, src_name, strip_in, ln, eps, out_name, strip_out,
                 out_bf16=False, sq_only=False):
    ctx.new_phase(f"NS_{out_name}")
    V, dr = ctx.V, ctx.dr
    rstd_end = emit_rstd_batch(ctx, strip_in, eps)
    src = dr[src_name]
    dst = dr[out_name]
    st = {}
    for c in range(NCH):
        ld = z_chunk_load(ctx, src, c, guard_dv=st.get(("n", c - 1)))
        lds = chunk_stage_load(ctx, strip_in, c, after=rstd_end if c == 0 else None,
                               after_st=st.get(("stm", c - 1), 0))
        evbc = emit_bcast(ctx, c, lds,
                          pe_guard=None if c == 0 else ("DV", st[("sx", c - 1)]))
        ctx.wait("vector", "LD", ld)
        zo = V["zo"][c % 2]
        wk = V["wk"]
        ctx.wait("vector", "ST", st.get(("st", c - 1)))
        ctx.wait("vector", "DV", st.get(("sx", c - 1)))
        ctx.wait("vector", "EV", evbc)
        dl = el = None
        _aff = {}
        for pt in range(NPT):
            d1 = ctx.run("vector", "DV",
                         lambda e, pt=pt, c=c: e.scalar_tensor_tensor(
                             out=wk[0], in0=ctx.P["mbc"], scalar=-1.0 / D,
                             in1=V["zin"][c % 2][:, pt, :], op0=OP.mult, op1=OP.add))
            ctx.wait("vector", "DV", d1)
            if ln is None:
                dl = ctx.run("vector", "DV",
                             lambda e, pt=pt, zo=zo: e.tensor_tensor(
                                 out=zo[:, pt, :], in0=wk[0], in1=ctx.P["rbc"], op=OP.mult))
            else:
                sj, bj, jj = ln
                wt = wk[1 + 3 * (pt % 2)]
                g = _aff.get(pt - 1)
                if g:
                    ctx.wait("vector", "EV", g)
                d2 = ctx.run("vector", "DV",
                             lambda e, wt=wt: e.tensor_tensor(out=wt, in0=wk[0],
                                                              in1=ctx.P["rbc"], op=OP.mult))
                ctx.wait("scalar", "DV", d2)
                el = ctx.run("scalar", "EV",
                             lambda e, pt=pt, zo=zo, sj=sj, bj=bj, jj=jj, wt=wt: e.activation(
                                 out=zo[:, pt, :], in_=wt, func=AF.Identity,
                                 scale=ctx.P[sj][:, jj * 8 + pt:jj * 8 + pt + 1],
                                 bias=ctx.P[bj][:, jj * 8 + pt:jj * 8 + pt + 1]))
                _aff[pt] = el
                dl = d2
        st[("n", c)] = dl
        ready = ("EV", el) if el is not None else ("DV", dl)
        dvx, stmk = emit_stats(ctx, c, zo, strip_out, ready=ready, sq_only=sq_only,
                               stm_prev=st.get(("stm", c - 1), 0))
        st[("sx", c)] = dvx
        st[("stm", c)] = stmk
        if out_bf16:
            ee = None
            ctx.wait("scalar", "ST", st.get(("st", c - 1)))
            if el is not None:
                ctx.wait("scalar", "EV", el)
            else:
                ctx.wait("scalar", "DV", dl)
            for pt in range(NPT):
                ee = ctx.run("scalar", "EV",
                             lambda e, pt=pt, zo=zo: e.activation(
                                 out=V["xnb"][:, pt, :], in_=zo[:, pt, :], func=AF.Copy))
            d = dst.rearrange("(pt p) t -> p pt t", p=128)[:, :, c * CH:(c + 1) * CH]
            ctx.wait("scalar", "EV", ee)
            st[("st", c)] = ctx.dma("scalar", d, V["xnb"], "ST")
        else:
            st[("st", c)] = z_chunk_store(ctx, dst, c,
                                          ready_dv=None if el else dl,
                                          ready_ev=el)
    ctx.wait("scalar", "ST", ctx.c["ST"])


def ph_tail(ctx, src_name, strip_in, base_name, out_name, strip_out, external=False):
    ctx.new_phase(f"TAIL_{out_name}")
    V, dr = ctx.V, ctx.dr
    rstd_end = emit_rstd_batch(ctx, strip_in, EPS_OUT)
    src = dr[src_name]
    base = dr[base_name]
    dst = dr[out_name]
    st = {}
    for c in range(NCH):
        ld = z_chunk_load(ctx, src, c, guard_dv=st.get(("n", c - 1)))
        bse = _v(ctx.arena, A_HB + (c % 2) * 8192, 0, 128, [NPT, CH], F32)
        if c < 2:
            ctx.wait("sync", "LD", rstd_end)
        else:
            ctx.wait("sync", "DV", st.get(("n", c - 1)))
        bsrc = base.rearrange("(pt p) t -> p pt t", p=128)[:, :, c * CH:(c + 1) * CH]
        ldb = ctx.dma("sync", bse, bsrc, "LD")
        lds = chunk_stage_load(ctx, strip_in, c, after=rstd_end if c == 0 else None,
                               after_st=st.get(("stm", c - 1), 0),
                               after_dv=st.get(("n", c - 1), 0) if external else 0)
        evbc = emit_bcast(ctx, c, lds,
                          pe_guard=None if c == 0 else ("DV", st[("sx", c - 1)]))
        ctx.wait("vector", "LD", max(ld, ldb))
        zo = V["zo"][c % 2]
        wk = V["wk"]
        ctx.wait("vector", "ST", st.get(("st", c - 1)))
        ctx.wait("vector", "DV", st.get(("sx", c - 1)))
        ctx.wait("vector", "EV", evbc)
        dl = None
        for pt in range(NPT):
            d1 = ctx.run("vector", "DV",
                         lambda e, pt=pt, c=c: e.scalar_tensor_tensor(
                             out=wk[0], in0=ctx.P["mbc"], scalar=-1.0 / D,
                             in1=V["zin"][c % 2][:, pt, :], op0=OP.mult, op1=OP.add))
            ctx.wait("vector", "DV", d1)
            d2 = ctx.run("vector", "DV",
                         lambda e: e.tensor_tensor(out=wk[1], in0=wk[0],
                                                   in1=ctx.P["rbc"], op=OP.mult))
            ctx.wait("vector", "DV", d2)
            dl = ctx.run("vector", "DV",
                         lambda e, pt=pt, zo=zo, bse=bse: e.tensor_tensor(
                             out=zo[:, pt, :], in0=wk[1], in1=bse[:, pt, :], op=OP.add))
        st[("n", c)] = dl
        if not external:
            dvx, stmk = emit_stats(ctx, c, zo, strip_out, ready=("DV", dl),
                                   stm_prev=st.get(("stm", c - 1), 0))
            st[("sx", c)] = dvx
            st[("stm", c)] = stmk
        else:
            st[("sx", c)] = dl
        st[("st", c)] = z_chunk_store(ctx, dst, c, ready_dv=dl)
    ctx.wait("scalar", "ST", ctx.c["ST"])


def ph_ternarize(ctx):
    V, P, dr = ctx.V, ctx.P, ctx.dr
    ctx.new_phase("TER_a")
    col = 0
    dv_done = {}
    for name, nt, w in [("f_wu", NPT, FF), ("f_wd", NFT, D)]:
        for i in range(nt):
            src = dr[name].rearrange("(n p) o -> n p o", p=128)[i]
            par = i % 2
            g = dv_done.get(col - 1)
            if g:
                ctx.wait("sync", "DV", g)
            ld = ctx.dma("sync", V["ter"][par][:, 0:w], src, "LD")
            ctx.wait("scalar", "LD", ld)
            e1 = ctx.run("scalar", "EV",
                         lambda e, par=par, w=w: e.activation(
                             out=V["terA"][:, 0:w], in_=V["ter"][par][:, 0:w], func=AF.Abs))
            ctx.wait("vector", "EV", e1)
            d1 = ctx.run("vector", "DV",
                         lambda e, col=col, w=w: e.tensor_scalar(
                             out=V["terA"][:, 0:w], in0=V["terA"][:, 0:w], scalar1=1.0,
                             scalar2=0.0, op0=OP.mult, op1=OP.add,
                             accum_out=P["kvp"][:, col:col + 1]))
            dv_done[col] = d1
            col += 1
    d_up = ctx.run("vector", "DV",
                   lambda e: e.reduce_sum(out=P["sk"][:, 0:1], in_=P["kvp"][:, 0:NPT],
                                          axis=mybir.AxisListType.X))
    d_dn = ctx.run("vector", "DV",
                   lambda e: e.reduce_sum(out=P["sk"][:, 1:2], in_=P["kvp"][:, NPT:NPT + NFT],
                                          axis=mybir.AxisListType.X))
    ctx.wait("tensor", "DV", d_dn)
    p1 = ctx.run("tensor", "PE",
                 lambda e: e.matmul(ctx.ps["pssb"][0:1, 0:2], V["onescol"],
                                    P["sk"][:, 0:2], start=True, stop=True,
                                    skip_group_check=True))
    ctx.wait("vector", "PE", p1)
    d2 = ctx.run("vector", "DV",
                 lambda e: e.tensor_copy(out=P["alpha"][0:1, 2:4], in_=ctx.ps["pssb"][0:1, 0:2]))
    ctx.wait("tensor", "DV", d2)
    p2 = ctx.run("tensor", "PE",
                 lambda e: e.matmul(ctx.ps["pssb"][:, 4:6], V["ones1f"],
                                    P["alpha"][0:1, 2:4], start=True, stop=True,
                                    skip_group_check=True))
    ctx.wait("vector", "PE", p2)
    ctx.run("vector", "DV",
            lambda e: e.tensor_scalar(out=P["alpha"][:, 0:2], in0=ctx.ps["pssb"][:, 4:6],
                                      scalar1=0.5 / (D * FF), scalar2=None, op0=OP.mult))

    ctx.new_phase("TER_b")
    col = 0
    st_done = {}
    for name, outn, nt, acol, w in [("f_wu", "wtu", NPT, 0, FF), ("f_wd", "wtd", NFT, 1, D)]:
        for i in range(nt):
            src = dr[name].rearrange("(n p) o -> n p o", p=128)[i]
            dstw = dr[outn].rearrange("(n p) o -> n p o", p=128)[i]
            par = i % 2
            g = st_done.get(col - 1)
            if g:
                ctx.wait("sync", "ST", g)
            ld = ctx.dma("sync", V["ter"][par][:, 0:w], src, "LD")
            ctx.wait("scalar", "LD", ld)
            e1 = ctx.run("scalar", "EV",
                         lambda e, par=par, w=w: e.activation(
                             out=V["terb"][par][:, 0:w], in_=V["ter"][par][:, 0:w],
                             func=AF.Sign))
            ctx.wait("scalar", "EV", e1)
            e2 = ctx.run("scalar", "EV",
                         lambda e, par=par, w=w: e.activation(
                             out=V["terA"][:, 0:w], in_=V["ter"][par][:, 0:w], func=AF.Abs))
            ctx.wait("vector", "EV", e2)
            d1 = ctx.run("vector", "DV",
                         lambda e, acol=acol, w=w: e.tensor_scalar(
                             out=V["terA"][:, 0:w], in0=V["terA"][:, 0:w],
                             scalar1=P["alpha"][:, acol:acol + 1], scalar2=None, op0=OP.is_gt))
            ctx.wait("vector", "DV", d1)
            d2 = ctx.run("vector", "DV",
                         lambda e, par=par, w=w: e.tensor_tensor(
                             out=V["terb"][par][:, 0:w], in0=V["terb"][par][:, 0:w],
                             in1=V["terA"][:, 0:w], op=OP.mult))
            ctx.wait("scalar", "DV", d2)
            st_done[col] = ctx.dma("scalar", dstw, V["terb"][par][:, 0:w], "ST")
            col += 1
    ctx.wait("scalar", "ST", ctx.c["ST"])


def ph_F2(ctx, strip_rms, out_name, strip_out):
    ctx.new_phase("F2")
    V, P, dr = ctx.V, ctx.P, ctx.dr
    rstd_end = emit_rstd_batch(ctx, strip_rms, 0.0, out_is_rx=True)
    wu = load_w_big(ctx, "wE_b", dr["wtu"], FF)
    wd = load_w_big(ctx, "wF_b", dr["wtd"], D)
    w1v = V["wE_b"].rearrange("p (ct o) -> p ct o", o=FF)
    w2v = V["wF_b"].rearrange("p (ct o) -> p ct o", o=D)
    src = dr["xif"]
    dst = dr[out_name]
    psw = [ctx.ps["psw0"], ctx.ps["psw1"]]
    psd = [ctx.ps["ps0"], ctx.ps["ps1"]]
    pssb = ctx.ps["pssb"]
    st = {}
    for c in range(NCH):
        g = st.get(("peuse", c - 1))
        if g:
            ctx.wait("sync", "PE", g)
        s = src.rearrange("(pt p) t -> p pt t", p=128)[:, :, c * CH:(c + 1) * CH]
        zb = _v(ctx.arena, A_ZIN0 + (c % 2) * 8192, 0, 128, [NPT, CH], BF16)
        ld = ctx.dma("sync", zb, s, "LD")
        lds = chunk_stage_load(ctx, strip_rms, c, after=rstd_end if c == 0 else None,
                               after_st=st.get(("stm", c - 1), 0))
        ctx.wait("tensor", "LD", lds)
        if c > 0:
            ctx.wait("tensor", "DV", st[("sx", c - 1)])
        pbc = ctx.run("tensor", "PE",
                      lambda e, c=c: e.matmul(pssb[:, CH:2 * CH], V["ones1f"],
                                              V["sti"][c % 2][0:1, CH:2 * CH],
                                              start=True, stop=True, skip_group_check=True))
        ctx.wait("scalar", "PE", pbc)
        evbc = ctx.run("scalar", "EV",
                       lambda e: e.activation(out=P["rbc"], in_=pssb[:, CH:2 * CH], func=AF.Copy))
        ctx.wait("tensor", "WT", wu)
        ctx.wait("tensor", "LD", ld)
        if c == 0:
            ctx.wait("scalar", "LD", rstd_end)
        for fp in range(NPT):
            par = fp % 2
            g = st.get(("hev", c * NPT + fp - 1))
            if g:
                ctx.wait("tensor", "EV", g)
            for ot in range(4):
                fo = fp * 4 + ot
                for ct in range(NPT):
                    ctx.run("tensor", "PE",
                            lambda e, fo=fo, ct=ct, par=par, ot=ot, zb=zb: e.matmul(
                                psw[par][:, ot * CH:(ot + 1) * CH],
                                w1v[:, ct, fo * 128:(fo + 1) * 128],
                                zb[:, ct, :], start=(ct == 0 and ot % 2 == 0),
                                stop=(ct == NPT - 1 and ot % 2 == 1),
                                skip_group_check=True))
            peh = ctx.c["PE"]
            lasts = {}
            for ot in range(4):
                fo = fp * 4 + ot
                wpar = ot % 2
                a = V["wk"][3 * wpar + 1]
                ctx.wait("vector", "PE", peh)
                ctx.wait("vector", "EV", evbc)
                g2 = lasts.get(ot - 1)
                if g2:
                    ctx.wait("vector", "EV", g2)
                d1 = ctx.run("vector", "DV",
                             lambda e, par=par, ot=ot, a=a: e.tensor_tensor(
                                 out=a, in0=psw[par][:, ot * CH:(ot + 1) * CH],
                                 in1=P["rbc"], op=OP.mult))
                ctx.wait("scalar", "DV", d1)
                e1 = ctx.run("scalar", "EV",
                             lambda e, fo=fo, a=a: e.activation(
                                 out=V["hb"][:, fo, :], in_=a, func=AF.Gelu))
                lasts[ot] = e1
            st[("hev", c * NPT + fp)] = lasts[3]
        ctx.wait("tensor", "EV", st[("hev", c * NPT + NPT - 1)])
        ctx.wait("tensor", "WT", wd)
        zo = V["zo"][c % 2]
        ctx.wait("vector", "ST", st.get(("zst", c - 1)))
        ctx.wait("vector", "DV", st.get(("sx", c - 1)))
        dvl = None
        for r in range(2):
            if r == 1:
                ctx.wait("tensor", "DV", dvl)
            for dt4 in range(4):
                dtile = r * 4 + dt4
                pd = psd[dt4 // 2]
                off = (dt4 % 2) * CH
                for ct in range(NFT):
                    ctx.run("tensor", "PE",
                            lambda e, dtile=dtile, ct=ct, pd=pd, off=off, dt4=dt4: e.matmul(
                                pd[:, off:off + CH], w2v[:, ct, dtile * 128:(dtile + 1) * 128],
                                V["hb"][:, ct, :], start=(ct == 0 and dt4 % 2 == 0),
                                stop=(ct == NFT - 1 and dt4 % 2 == 1),
                                skip_group_check=True))
            ped = ctx.c["PE"]
            for dt4 in range(4):
                dtile = r * 4 + dt4
                pd = psd[dt4 // 2]
                off = (dt4 % 2) * CH
                ctx.wait("vector", "PE", ped)
                dvl = ctx.run("vector", "DV",
                              lambda e, pd=pd, off=off, dtile=dtile, zo=zo: e.tensor_copy(
                                  out=zo[:, dtile, :], in_=pd[:, off:off + CH]))
        st[("peuse", c)] = ctx.c["PE"]
        dvx, stmk = emit_stats(ctx, c, zo, strip_out, ready=("DV", dvl),
                               stm_prev=st.get(("stm", c - 1), 0))
        st[("sx", c)] = dvx
        st[("stm", c)] = stmk
        st[("zst", c)] = z_chunk_store(ctx, dst, c, ready_dv=dvl)
    ctx.wait("scalar", "ST", ctx.c["ST"])


def ph_P1(ctx, strip_in):
    ctx.new_phase("P1")
    V, dr = ctx.V, ctx.dr
    rstd_end = emit_rstd_batch(ctx, strip_in, EPS_OUT)
    lw = [load_w_small(ctx, 0, dr["a_wq"]), load_w_small(ctx, 1, dr["a_wk"]),
          load_w_small(ctx, 2, dr["a_wv"])]
    wv = [V["ws"][k].rearrange("p (ct o) -> p ct o", o=D) for k in range(3)]
    src = dr["xo2"]
    dsts = [dr["xq"], dr["xk"], dr["xv"]]
    psp = [ctx.ps["ps0"], ctx.ps["ps1"]]
    st = {}
    for c in range(NCH):
        ld = z_chunk_load(ctx, src, c, guard_dv=st.get(("n", c - 1)))
        lds = chunk_stage_load(ctx, strip_in, c, after=rstd_end if c == 0 else None,
                               after_dv=st.get(("n", c - 2), 0))
        evbc = emit_bcast(ctx, c, lds,
                          pe_guard=None if c == 0 else ("DV", st[("n", c - 1)]))
        ctx.wait("vector", "LD", ld)
        if c == 0:
            ctx.wait("scalar", "LD", rstd_end)
        dvn, _ = emit_normalize(ctx, c, evbc, None, want_f32=False, want_bf16=True)
        st[("n", c)] = dvn
        ctx.wait("tensor", "WT", lw[2])
        ctx.wait("tensor", "DV", dvn)
        for wsl in range(3):
            idx = c * 3 + wsl
            pjb = V["pj"][idx % 2]
            g = st.get(("pst", idx - 1))
            if g:
                ctx.wait("scalar", "ST", g)
            eel = None
            for m in range(NPT):
                par = m % 2
                g2 = st.get(("ev", idx * NPT + m - 1))
                if g2:
                    ctx.wait("tensor", "EV", g2)
                for ct in range(NPT):
                    ctx.run("tensor", "PE",
                            lambda e, wsl=wsl, m=m, ct=ct, par=par: e.matmul(
                                psp[par][:, 0:CH], wv[wsl][:, ct, m * 128:(m + 1) * 128],
                                V["xnb"][:, ct, :], start=(ct == 0), stop=(ct == NPT - 1),
                                skip_group_check=True))
                pe = ctx.c["PE"]
                ctx.wait("scalar", "PE", pe)
                eel = ctx.run("scalar", "EV",
                              lambda e, m=m, par=par, pjb=pjb: e.activation(
                                  out=pjb[:, m, :], in_=psp[par][:, 0:CH], func=AF.Copy))
                st[("ev", idx * NPT + m)] = eel
            d = dsts[wsl].rearrange("(pt p) t -> p pt t", p=128)[:, :, c * CH:(c + 1) * CH]
            ctx.wait("scalar", "EV", eel)
            st[("pst", idx)] = ctx.dma("scalar", d, pjb, "ST")
    ctx.wait("scalar", "ST", ctx.c["ST"])


def ph_P2(ctx):
    ctx.new_phase("P2")
    V, dr = ctx.V, ctx.dr
    attT = V["wE_b"].rearrange("p (ct t) -> p ct t", t=T)
    psS = [ctx.ps["ps0"], ctx.ps["ps1"]]
    psA = ctx.ps["ps2"]
    pssb = ctx.ps["pssb"]
    NQT = T // 512
    exp_k = {}
    av_k = {}
    attw_k = {}
    last_dw = None
    xk_r = dr["xk"].rearrange("(fb d) t -> d fb t", d=64)
    xq_r = dr["xq"].rearrange("(fb d) t -> d fb t", d=64)
    for h in range(16):
        par = h % 2
        if h >= 2:
            ctx.wait("sync", "DV", attw_k[h - 1])
        lk = ctx.dma("sync", V["kh"][par].rearrange("d (fb t) -> d fb t", t=256),
                     xk_r[:, :, 256 * h:256 * (h + 1)], "LD")
        lq = ctx.dma("sync", V["qh"][par].rearrange("d (fb t) -> d fb t", t=256),
                     xq_r[:, :, 256 * h:256 * (h + 1)], "LD")
        for kt in range(32):
            fb, half = kt // 2, kt % 2
            ctx.dma("sync", V["v1"][par][:, kt, 0:64],
                    dr["xv"][fb * 64:(fb + 1) * 64,
                             256 * h + half * 128:256 * h + (half + 1) * 128],
                    "LD", transpose=True)
        lv = ctx.c["LD"]
        gv = ctx.run("gpsimd", "GP",
                     lambda e, par=par: e.memset(V["v1"][par][:, :, 64:65], 1.0))
        kh = V["kh"][par].rearrange("p (kt x) -> p kt x", x=128)
        qh = V["qh"][par]
        for qt in range(NQT):
            qi = h * NQT + qt
            for kt in range(32):
                spar = kt % 2
                ctx.wait("tensor", "LD", max(lk, lq, lv))
                prev = exp_k.get(qi * 32 + kt - 1)
                if prev:
                    ctx.wait("tensor", "EV", prev)
                if kt == 0 and last_dw:
                    ctx.wait("tensor", "DV", last_dw)
                pe_s = ctx.run("tensor", "PE",
                               lambda e, kt=kt, spar=spar, qt=qt, kh=kh, qh=qh: e.matmul(
                                   psS[spar][:, 0:512], kh[:, kt, :],
                                   qh[:, qt * 512:(qt + 1) * 512],
                                   start=True, stop=True, skip_group_check=True))
                ctx.wait("scalar", "PE", pe_s)
                pav = av_k.get(qi * 32 + kt - 1)
                if pav:
                    ctx.wait("scalar", "PE", pav)
                ee = ctx.run("scalar", "EV",
                             lambda e, spar=spar, kt=kt: e.activation(
                                 out=V["ptb"][kt % 2], in_=psS[spar][:, 0:512],
                                 func=AF.Exp, scale=SCALE_ATT))
                exp_k[qi * 32 + kt] = ee
                ctx.wait("tensor", "EV", ee)
                if kt == 0:
                    ctx.wait("tensor", "GP", gv)
                pe_a = ctx.run("tensor", "PE",
                               lambda e, kt=kt, par=par: e.matmul(
                                   psA[0:65, 0:512], V["v1"][par][:, kt, 0:65],
                                   V["ptb"][kt % 2], start=(kt == 0), stop=(kt == 31),
                                   skip_group_check=True))
                av_k[qi * 32 + kt] = pe_a
            ctx.wait("vector", "PE", av_k[qi * 32 + 31])
            d1 = ctx.run("vector", "DV",
                         lambda e: e.reciprocal(out=V["rbd"], in_=psA[64:65, 0:512]))
            ctx.wait("tensor", "DV", d1)
            pbc = ctx.run("tensor", "PE",
                          lambda e: e.matmul(pssb[:, 0:512], V["ones1f64"],
                                             V["rbd"], start=True, stop=True,
                                             skip_group_check=True))
            ctx.wait("scalar", "PE", pbc)
            ebc = ctx.run("scalar", "EV",
                          lambda e: e.activation(out=V["rb"], in_=pssb[0:64, 0:512],
                                                 func=AF.Copy))
            ctx.wait("vector", "EV", ebc)
            for sub in range(2):
                fb = 2 * qt + sub
                dw = ctx.run("vector", "DV",
                             lambda e, h=h, fb=fb, sub=sub: e.tensor_tensor(
                                 out=attT[(fb % 2) * 64:(fb % 2) * 64 + 64, fb // 2,
                                          256 * h:256 * (h + 1)],
                                 in0=psA[0:64, sub * 256:(sub + 1) * 256],
                                 in1=V["rb"][:, sub * 256:(sub + 1) * 256], op=OP.mult))
            attw_k[h] = dw
            last_dw = dw


def ph_P3(ctx, strip_out):
    ctx.new_phase("P3")
    V, dr = ctx.V, ctx.dr
    lwo = load_w_small(ctx, 3, dr["a_wo"])
    wo3 = V["ws"][3].rearrange("p (ct o) -> p ct o", o=D)
    attT = V["wE_b"].rearrange("p (ct t) -> p ct t", t=T)
    dst = dr["zA"]
    psp = [ctx.ps["ps0"], ctx.ps["ps1"]]
    st = {}
    for c in range(NCH):
        ctx.wait("tensor", "WT", lwo)
        zo = V["zo"][c % 2]
        dvl = None
        for m2 in range(NPT):
            par = m2 % 2
            g = st.get(("ev", c * NPT + m2 - 1))
            if g:
                ctx.wait("tensor", "DV", g)
            for ct in range(NPT):
                ctx.run("tensor", "PE",
                        lambda e, m2=m2, ct=ct, par=par, c=c: e.matmul(
                            psp[par][:, 0:CH], wo3[:, ct, m2 * 128:(m2 + 1) * 128],
                            attT[:, ct, c * CH:(c + 1) * CH],
                            start=(ct == 0), stop=(ct == NPT - 1), skip_group_check=True))
            pe = ctx.c["PE"]
            ctx.wait("vector", "PE", pe)
            if m2 == 0:
                ctx.wait("vector", "ST", st.get(("st", c - 1)))
                ctx.wait("vector", "DV", st.get(("sx", c - 1)))
            dvl = ctx.run("vector", "DV",
                          lambda e, m2=m2, par=par, zo=zo: e.tensor_copy(
                              out=zo[:, m2, :], in_=psp[par][:, 0:CH]))
            st[("ev", c * NPT + m2)] = dvl
        if c > 0:
            ctx.wait("tensor", "DV", st[("sx", c - 1)])
        dvx, stmk = emit_stats(ctx, c, zo, strip_out, ready=("DV", dvl),
                               stm_prev=st.get(("stm", c - 1), 0))
        st[("sx", c)] = dvx
        st[("stm", c)] = stmk
        st[("st", c)] = z_chunk_store(ctx, dst, c, ready_dv=dvl)
    ctx.wait("scalar", "ST", ctx.c["ST"])


def ph_P5(ctx, strip_in, out_name, strip_out):
    ctx.new_phase("P5")
    V, P, dr = ctx.V, ctx.P, ctx.dr
    rstd_end = emit_rstd_batch(ctx, strip_in, EPS_OUT)
    lws = [load_w_small(ctx, 0, dr["r_wr"]), load_w_small(ctx, 1, dr["r_wk"]),
           load_w_small(ctx, 2, dr["r_wv"])]
    wvv = [V["ws"][k].rearrange("p (ct o) -> p ct o", o=D) for k in range(3)]
    src = dr["xo3"]
    dst = dr[out_name]
    psR = ctx.ps["psw0"]
    psK = [ctx.ps["ps0"], ctx.ps["ps1"]]
    psV2 = ctx.ps["psw1"]
    st = {}
    for c in range(NCH):
        ld = z_chunk_load(ctx, src, c, guard_dv=st.get(("mx", c - 1)))
        lds = chunk_stage_load(ctx, strip_in, c, after=rstd_end if c == 0 else None,
                               after_st=st.get(("stm", c - 1), 0))
        evbc = emit_bcast(ctx, c, lds,
                          pe_guard=None if c == 0 else ("DV", st[("sx", c - 1)]))
        ctx.wait("vector", "LD", ld)
        dvn, _ = emit_normalize(ctx, c, evbc, None, want_f32=True, want_bf16=False)
        ctx.wait("gpsimd", "DV", dvn)
        if c == 0:
            ctx.wait("gpsimd", "LD", rstd_end)
        g3 = None
        for pt in range(NPT):
            ctx.run("gpsimd", "GP",
                    lambda e, pt=pt: e.tensor_copy(out=V["xi4sh"][:, pt, 1:CH],
                                                   in_=V["xn"][:, pt, 0:CH - 1]))
            ctx.run("gpsimd", "GP",
                    lambda e, pt=pt: e.tensor_copy(out=V["xi4sh"][:, pt, 0:1],
                                                   in_=P["pl"][:, pt:pt + 1]))
            g3 = ctx.run("gpsimd", "GP",
                         lambda e, pt=pt: e.tensor_copy(out=P["pl"][:, pt:pt + 1],
                                                        in_=V["xn"][:, pt, CH - 1:CH]))
        ctx.wait("vector", "GP", g3)
        dm = None
        for pt in range(NPT):
            d1 = ctx.run("vector", "DV",
                         lambda e, pt=pt: e.tensor_tensor(
                             out=V["wk"][0], in0=V["xn"][:, pt, :], in1=V["xi4sh"][:, pt, :],
                             op=OP.subtract))
            ctx.wait("vector", "DV", d1)
            ctx.run("vector", "DV",
                    lambda e, pt=pt: e.scalar_tensor_tensor(
                        out=V["xnb"][:, pt, :], in0=V["wk"][0],
                        scalar=P["mur"][:, pt:pt + 1], in1=V["xi4sh"][:, pt, :],
                        op0=OP.mult, op1=OP.add))
            dm = ctx.run("vector", "DV",
                         lambda e, pt=pt: e.scalar_tensor_tensor(
                             out=V["xkb5"][:, pt, :], in0=V["wk"][0],
                             scalar=P["muk"][:, pt:pt + 1], in1=V["xi4sh"][:, pt, :],
                             op0=OP.mult, op1=OP.add))
        st[("mx", c)] = dm
        ctx.wait("tensor", "WT", lws[2])
        ctx.wait("tensor", "DV", dm)
        kaev = None
        for m in range(NPT):
            par = m % 2
            g = st.get(("kaev", c * NPT + m - 1))
            if g:
                ctx.wait("tensor", "EV", g)
            for ct in range(NPT):
                ctx.run("tensor", "PE",
                        lambda e, m=m, ct=ct, par=par: e.matmul(
                            psK[par][:, 0:CH], wvv[1][:, ct, m * 128:(m + 1) * 128],
                            V["xkb5"][:, ct, :], start=(ct == 0), stop=(ct == NPT - 1),
                            skip_group_check=True))
            pe = ctx.c["PE"]
            ctx.wait("scalar", "PE", pe)
            e1 = ctx.run("scalar", "EV",
                         lambda e, par=par: e.activation(out=V["wk"][1],
                                                         in_=psK[par][:, 0:CH], func=AF.Relu))
            ctx.wait("scalar", "EV", e1)
            kaev = ctx.run("scalar", "EV",
                           lambda e, m=m: e.activation(out=V["ka5"][:, m, :],
                                                       in_=V["wk"][1], func=AF.Square))
            st[("kaev", c * NPT + m)] = kaev
        ctx.wait("tensor", "EV", kaev)
        zo = V["zo"][c % 2]
        ctx.wait("vector", "ST", st.get(("zst", c - 1)))
        ctx.wait("vector", "DV", st.get(("sx", c - 1)))
        dvl = None
        for m in range(NPT):
            parv = m % 2
            g = st.get(("odv", c * NPT + m - 1))
            if g:
                ctx.wait("tensor", "DV", g)
            for ct in range(NPT):
                ctx.run("tensor", "PE",
                        lambda e, m=m, ct=ct, parv=parv: e.matmul(
                            psV2[:, parv * 512:parv * 512 + CH],
                            wvv[2][:, ct, m * 128:(m + 1) * 128],
                            V["ka5"][:, ct, :], start=(ct == 0), stop=(ct == NPT - 1),
                            skip_group_check=True))
            pev = ctx.c["PE"]
            for ct in range(NPT):
                ctx.run("tensor", "PE",
                        lambda e, m=m, ct=ct, parv=parv: e.matmul(
                            psR[:, parv * 512:parv * 512 + CH],
                            wvv[0][:, ct, m * 128:(m + 1) * 128],
                            V["xnb"][:, ct, :], start=(ct == 0), stop=(ct == NPT - 1),
                            skip_group_check=True))
            per = ctx.c["PE"]
            ctx.wait("scalar", "PE", per)
            g2 = st.get(("odv", c * NPT + m - 1))
            if g2:
                ctx.wait("scalar", "DV", g2)
            e1 = ctx.run("scalar", "EV",
                         lambda e, parv=parv: e.activation(
                             out=V["wk"][2], in_=psR[:, parv * 512:parv * 512 + CH],
                             func=AF.Sigmoid))
            ctx.wait("vector", "EV", e1)
            ctx.wait("vector", "PE", pev)
            dvl = ctx.run("vector", "DV",
                          lambda e, m=m, parv=parv, zo=zo: e.tensor_tensor(
                              out=zo[:, m, :], in0=V["wk"][2],
                              in1=psV2[:, parv * 512:parv * 512 + CH], op=OP.mult))
            st[("odv", c * NPT + m)] = dvl
        dvx, stmk = emit_stats(ctx, c, zo, strip_out, ready=("DV", dvl),
                               stm_prev=st.get(("stm", c - 1), 0))
        st[("sx", c)] = dvx
        st[("stm", c)] = stmk
        st[("zst", c)] = z_chunk_store(ctx, dst, c, ready_dv=dvl)
    ctx.wait("scalar", "ST", ctx.c["ST"])


def ph_tap(ctx, src_name, tap_name):
    ctx.new_phase(f"tap_{tap_name}")
    src = ctx.dr[src_name]
    dst = ctx.taps[tap_name]
    stm = {}
    for c in range(NCH):
        g = stm.get(c - 2)
        if g:
            ctx.wait("sync", "ST", g)
        ld = z_chunk_load(ctx, src, c)
        ctx.wait("scalar", "LD", ld)
        d = dst.rearrange("(pt p) t -> p pt t", p=128)[:, :, c * CH:(c + 1) * CH]
        stm[c] = ctx.dma("scalar", d, ctx.V["zin"][c % 2], "ST")
    ctx.wait("scalar", "ST", ctx.c["ST"])


def make_phases(ctx, upto=None):
    ph_prep(ctx)
    r_x = ctx.strip_row()
    ph_stats0(ctx, "xt", r_x)

    stop = {"v": False}

    def done(tag):
        if upto is not None and tag == upto:
            stop["v"] = True
        return stop["v"]

    def maybe_tap(stream):
        if stream in ctx.taps:
            ph_tap(ctx, stream, stream)

    src, strip = "xt", r_x
    ln = None
    for j in range(NL):
        ph_A(ctx, j, src, strip, ln, do_batch=True)
        if done(f"A{j}"):
            return
        r_z1 = ctx.strip_row()
        ph_B1(ctx, j, src, strip, ln, r_z1, "zA")
        maybe_tap("zA")
        if done(f"B1_{j}"):
            return
        r_z2 = ctx.strip_row()
        ph_B2(ctx, j, "zA", r_z1, r_z2, "zB")
        maybe_tap("zB")
        if done(f"B2_{j}"):
            return
        src, strip = "zB", r_z2
        ln = ("l2s", "l2b", j)

    r_xi8 = ctx.strip_row()
    ph_normstore(ctx, src, strip, ln, EPS_IN, "zA", r_xi8)
    if done("T0a"):
        return
    r_xo1 = ctx.strip_row()
    ph_tail(ctx, "zA", r_xi8, "xt", "xo1", r_xo1)
    maybe_tap("xo1")
    if done("T0b"):
        return

    ph_ternarize(ctx)
    if done("TER"):
        return
    r_rms = ctx.strip_row()
    ph_normstore(ctx, "xo1", r_xo1, None, EPS_OUT, "xif", r_rms,
                 out_bf16=True, sq_only=True)
    if done("F1"):
        return
    r_dr = ctx.strip_row()
    ph_F2(ctx, r_rms, "zB", r_dr)
    if done("F2"):
        return
    r_xo2 = ctx.strip_row()
    ph_tail(ctx, "zB", r_dr, "xo1", "xo2", r_xo2)
    maybe_tap("xo2")
    if done("F3"):
        return

    ph_P1(ctx, r_xo2)
    if done("P1"):
        return
    ph_P2(ctx)
    if done("P2"):
        return
    r_y = ctx.strip_row()
    ph_P3(ctx, r_y)
    if done("P3"):
        return
    r_xo3 = ctx.strip_row()
    ph_tail(ctx, "zA", r_y, "xo2", "xo3", r_xo3)
    maybe_tap("xo3")
    if done("P4"):
        return

    r_out = ctx.strip_row()
    ph_P5(ctx, r_xo3, "zB", r_out)
    maybe_tap("zB2")
    if done("P5"):
        return
    ph_tail(ctx, "zB", r_out, "xo3", "y", None, external=True)


def build_nc(upto=None, taps=()):
    nc = bass.Bass("TRN2", target_bir_lowering=False, debug=False)
    dr = {}
    dr["xt"] = nc.dram_tensor("xt", [D, T], F32, kind="ExternalInput").ap()
    for n, shp in [("l0_wq", [8, D, D]), ("l0_wk", [8, D, D]), ("l0_wv", [8, D, D]),
                   ("l0_wo", [8, D, D]), ("l0_w1", [8, D, FF]), ("l0_w2", [8, FF, D]),
                   ("l0_bq", [8, 128, 8]), ("l0_bk", [8, 128, 8]), ("l0_bv", [8, 128, 8]),
                   ("l0_bo", [8, 128, 8]), ("l0_l1s", [8, 128, 8]), ("l0_l1b", [8, 128, 8]),
                   ("l0_l2s", [8, 128, 8]), ("l0_l2b", [8, 128, 8]),
                   ("l0_b1", [8, 128, 32]), ("l0_b2", [8, 128, 8]),
                   ("f_wu", [D, FF]), ("f_wd", [FF, D]),
                   ("a_wq", [D, D]), ("a_wk", [D, D]), ("a_wv", [D, D]), ("a_wo", [D, D]),
                   ("r_wr", [D, D]), ("r_wk", [D, D]), ("r_wv", [D, D]),
                   ("r_mur", [128, 8]), ("r_muk", [128, 8])]:
        dr[n] = nc.dram_tensor(n, shp, F32, kind="ExternalInput").ap()
    dr["y"] = nc.dram_tensor("y", [D, T], F32, kind="ExternalOutput").ap()
    for n, shp, dt in [("zA", [D, T], F32), ("zB", [D, T], F32),
                       ("xo1", [D, T], F32), ("xo2", [D, T], F32), ("xo3", [D, T], F32),
                       ("xif", [D, T], BF16), ("xq", [D, T], BF16), ("xk", [D, T], BF16),
                       ("xv", [D, T], BF16), ("wtu", [D, FF], BF16), ("wtd", [FF, D], BF16),
                       ("strips", [32, 2 * T], F32)]:
        dr[n] = nc.dram_tensor(n, shp, dt, kind="Internal").ap()
    taps_d = {}
    for tp in taps:
        taps_d[tp] = nc.dram_tensor("tap_" + tp, [D, T], F32, kind="ExternalOutput").ap()

    import contextlib
    with contextlib.ExitStack() as _stk:
        en = _stk.enter_context
        arena = en(nc.sbuf_tensor("arena", [128, ARENA_BYTES], U8))
        stg = en(nc.sbuf_tensor("stg", [128, 4096], U8))
        pp = en(nc.sbuf_tensor("pp", [128, 7424], U8))
        ps0 = en(nc.psum_tensor("ps0", [128, 512], F32))
        ps1 = en(nc.psum_tensor("ps1", [128, 512], F32))
        ps2 = en(nc.psum_tensor("ps2", [128, 512], F32))
        psw0 = en(nc.psum_tensor("psw0", [128, 1024], F32))
        psw1 = en(nc.psum_tensor("psw1", [128, 1024], F32))
        pssb = en(nc.psum_tensor("pssb", [128, 512], F32))
        sLD = en(nc.semaphore("sLD")); sWT = en(nc.semaphore("sWT"))
        sST = en(nc.semaphore("sST")); sPE = en(nc.semaphore("sPE"))
        sEV = en(nc.semaphore("sEV")); sDV = en(nc.semaphore("sDV"))
        sGP = en(nc.semaphore("sGP"))
        bSY = en(nc.semaphore("bSY")); bTE = en(nc.semaphore("bTE"))
        bSC = en(nc.semaphore("bSC")); bVE = en(nc.semaphore("bVE"))
        bGP = en(nc.semaphore("bGP"))
        ctx = Ctx(nc)
        ctx.sems = {"LD": sLD, "WT": sWT, "ST": sST, "PE": sPE,
                    "EV": sEV, "DV": sDV, "GP": sGP}
        ctx.bsems = {"sync": bSY, "tensor": bTE, "scalar": bSC,
                     "vector": bVE, "gpsimd": bGP}
        ctx.dr = dr
        ctx.taps = taps_d
        ctx.ps = dict(ps0=ps0, ps1=ps1, ps2=ps2, psw0=psw0, psw1=psw1, pssb=pssb)
        ctx.arena = arena

        V = {}
        V["zin"] = [_v(arena, A_ZIN0, 0, 128, [NPT, CH], F32),
                    _v(arena, A_ZIN1, 0, 128, [NPT, CH], F32)]
        V["xn"] = _v(arena, A_XN, 0, 128, [NPT, CH], F32)
        V["xnb"] = _v(arena, A_XNB, 0, 128, [NPT, CH], BF16)
        V["zbf"] = _v(arena, A_ZBF, 0, 128, [2, 2 * CH], BF16)
        V["zo"] = [_v(arena, A_ZO0, 0, 128, [NPT, CH], F32),
                   _v(arena, A_ZO1, 0, 128, [NPT, CH], F32)]
        V["hb"] = _v(arena, A_HB, 0, 128, [NFT, CH], BF16)
        V["wk"] = [_v(arena, A_WK + i * 1024, 0, 128, [CH], F32) for i in range(6)]
        V["wE_b"] = _v(arena, A_WE, 0, 128, [8 * FF], BF16)
        V["wF_b"] = _v(arena, A_WF, 0, 128, [NFT * D], BF16)
        V["ws"] = [_v(arena, A_WF + k * 16384, 0, 128, [8 * D], BF16) for k in range(4)]
        V["attb"] = _v(arena, A_HB, 0, 128, [NPT, CH], BF16)
        V["rsq"] = [_v(arena, A_HB + i * 4096, 0, 1, [1024], F32) for i in range(4)]
        V["kh"] = [_v(arena, A_HB + i * 8192, 0, 64, [T], BF16) for i in range(2)]
        V["qh"] = [_v(arena, A_ZIN0 + i * 8192, 0, 64, [T], BF16) for i in range(2)]
        V["v1"] = [_v(arena, A_ZO0 + i * 5120, 0, 128, [32, 80], BF16) for i in range(2)]
        V["ptb"] = [_v(arena, A_ZO0 + 10240 + i * 1024, 0, 128, [512], BF16) for i in range(2)]
        V["rb"] = _v(arena, A_ZO0 + 12288, 0, 64, [512], F32)
        V["rbd"] = _v(arena, A_ZO0 + 12288, 64, 65, [512], F32)
        V["pj"] = [_v(arena, A_HB + i * 4096, 0, 128, [NPT, CH], BF16) for i in range(2)]
        V["xi4sh"] = _v(arena, A_HB, 0, 128, [NPT, CH], F32)
        V["xkb5"] = _v(arena, A_HB + 8192, 0, 128, [NPT, CH], BF16)
        V["ka5"] = _v(arena, A_HB + 12288, 0, 128, [NPT, CH], BF16)
        V["ter"] = [_v(arena, A_WE + i * 16384, 0, 128, [4096], F32) for i in range(2)]
        V["terA"] = _v(arena, A_WE + 32768, 0, 128, [4096], F32)
        V["terb"] = [_v(arena, A_WE + 49152 + i * 8192, 0, 128, [4096], BF16) for i in range(2)]

        stgf = stg.bitcast(F32)
        V["sti"] = [stgf[0:1, i * 512:(i + 1) * 512] for i in range(2)]
        V["stp"] = [stgf[32:33, i * 512:(i + 1) * 512] for i in range(2)]

        ppf = pp.bitcast(F32)
        poff = [0]

        def palloc(n):
            o = poff[0]
            poff[0] += n
            return ppf[:, o:o + n]
        P = {}
        for nm in ("bq", "bk", "bv", "bo", "l1s", "l1b", "l2s", "l2b"):
            P[nm] = palloc(64)
        P["b1"] = palloc(256)
        P["b2"] = palloc(64)
        P["mur"] = palloc(8)
        P["muk"] = palloc(8)
        P["kv"] = palloc(8)
        P["sk"] = palloc(8)
        P["kvp"] = palloc(128)
        P["skp"] = palloc(128)
        P["alpha"] = palloc(4)
        P["mbc"] = palloc(256)
        P["rbc"] = palloc(256)
        P["pl"] = palloc(8)
        b0 = poff[0]
        V["ones128b"] = pp.bitcast(BF16)[:, 2 * b0: 2 * b0 + 128]
        poff[0] += 64
        V["onescol"] = palloc(1)
        _of = palloc(128)
        V["ones1f_all"] = _of
        V["ones1f"] = _of[0:1, :]
        V["ones1f64"] = _of[64:65, :]
        assert poff[0] * 4 <= 7424, poff[0] * 4
        ctx.V = V
        ctx.P = P

        make_phases(ctx, upto=upto)
        ctx.finalize()

        with nc.Block() as block:
            _bar = {"sync": ("LD",), "tensor": ("PE",), "scalar": ("EV", "ST"),
                    "vector": ("DV",), "gpsimd": ("GP", "WT")}

            def section(name):
                def body(eng):
                    for ph in ctx.phases:
                        i = ph["idx"]
                        for o in ENGS:
                            if o != name:
                                eng.wait_ge(ctx.bsems[o], i)
                        for th in ph[name]:
                            th(eng)
                        for sm in _bar[name]:
                            v = ph["end"][sm]
                            if v > 0:
                                eng.wait_ge(ctx.sems[sm], v)
                        eng.nop().then_inc(ctx.bsems[name], 1)
                return body
            block.sync(section("sync"))
            block.tensor(section("tensor"))
            block.scalar(section("scalar"))
            block.vector(section("vector"))
            block.gpsimd(section("gpsimd"))
    return nc


_NC_CACHE = {}


KERNEL_MODE = "spmd8"


def kernel(**inputs):
    lay = host_layout(inputs)
    key = "full"
    if key not in _NC_CACHE:
        _NC_CACHE[key] = build_nc()
    nc = _NC_CACHE[key]
    if KERNEL_MODE == "spmd8":
        in_maps = []
        for b in range(8):
            m = {k: v for k, v in lay.items() if k != "xt"}
            m["xt"] = lay["xt"][b]
            in_maps.append(m)
        res = run_bass_kernel_spmd(nc, in_maps, core_ids=list(range(8)))
        out = np.stack([np.asarray(r["y"]).T for r in res.results], axis=0)
    else:
        outs = []
        for b in range(8):
            m = {k: v for k, v in lay.items() if k != "xt"}
            m["xt"] = lay["xt"][b]
            res = run_bass_kernel_spmd(nc, [m], core_ids=[0])
            outs.append(np.asarray(res.results[0]["y"]).T)
        out = np.stack(outs, axis=0)
    out = np.ascontiguousarray(out).astype(np.float32)
    try:
        np.save("/tmp/_k_last_out.npy", out)
    except Exception:
        pass
    return out
